# version 02
# baseline (speedup 1.0000x reference)
import math
from contextlib import ExitStack

import numpy as np
import concourse.bass as bass
import concourse.mybir as mybir
from concourse.bass_utils import run_bass_kernel_spmd

F32 = mybir.dt.float32
BF16 = mybir.dt.bfloat16
AF = mybir.ActivationFunctionType
OP = mybir.AluOpType

B, S, DM, HQ, HKV, DH, DFF = 4, 2048, 4096, 32, 8, 128, 11008
DC = DM // 128
FC = DFF // 128
PAIR_SLOTS = [[0, 3, 4, 7, 8, 11, 12, 15], [1, 2, 5, 6, 9, 10, 13, 14]]
SCALE = 1.0 / math.sqrt(128.0)
EPS = 1e-5


class Ctr:
    def __init__(self, sem):
        self.sem = sem
        self.v = 0

    def inc(self, ins, n=1):
        ins.then_inc(self.sem, n)
        self.v += n
        return ins


def build_program(upto="full", taps=()):
    taps = set(taps)
    nc = bass.Bass("TRN2", target_bir_lowering=False, debug=False)
    dt = nc.dram_tensor
    xt = dt("xt", [DM, S], F32, kind="ExternalInput").ap()
    xto = dt("xto", [DM, 1024], F32, kind="ExternalInput").ap()
    wkv = dt("wkv", [128, DC, 512], F32, kind="ExternalInput").ap()
    wku = dt("wku", [128, 4, 1024], F32, kind="ExternalInput").ap()
    wvu = dt("wvu", [128, 4, 1024], F32, kind="ExternalInput").ap()
    wq = dt("wq", [HQ, 128, DM], F32, kind="ExternalInput").ap()
    wo = dt("wo", [DC, 128, DM], F32, kind="ExternalInput").ap()
    wg = dt("wg", [FC, 128, DM], F32, kind="ExternalInput").ap()
    wu = dt("wu", [FC, 128, DM], F32, kind="ExternalInput").ap()
    wd = dt("wd", [DC, 128, DFF], F32, kind="ExternalInput").ap()
    maskd = dt("mask", [128, 2048], F32, kind="ExternalInput").ap()
    yt = dt("yt", [DC, 128, 1024], F32, kind="ExternalOutput").ap()
    x1sp = dt("x1sp", [128, DC * 1024], F32, kind="Internal").ap()

    tap_t = {}
    if "G" in taps:
        tap_t["G"] = dt("tap_G", [128, 4 * 2048], F32, kind="ExternalOutput").ap()
    if "K" in taps:
        tap_t["K"] = dt("tap_K", [128, HKV * 2048], F32, kind="ExternalOutput").ap()
    if "V" in taps:
        tap_t["V"] = dt("tap_V", [128, 16 * 1024], F32, kind="ExternalOutput").ap()
    for nm in ("ao0", "ao1"):
        if nm in taps:
            tap_t[nm] = dt("tap_" + nm, [128, HQ * 512], F32, kind="ExternalOutput").ap()
    for nm in ("x1_0", "x1_1"):
        if nm in taps:
            tap_t[nm] = dt("tap_" + nm, [128, DC * 512], F32, kind="ExternalOutput").ap()
    for nm in ("u1pre_0", "u1pre_1"):
        if nm in taps:
            tap_t[nm] = dt("tap_" + nm, [128, DC * 512], F32, kind="ExternalOutput").ap()
    for nm in ("stat_0", "stat_1"):
        if nm in taps:
            tap_t[nm] = dt("tap_" + nm, [1, 512 * 5], F32, kind="ExternalOutput").ap()

    es = ExitStack()
    with es:
        sem = lambda n: es.enter_context(nc.semaphore(n))
        cPE, cACT, cDVE, cPOOL = Ctr(sem("cPE")), Ctr(sem("cACT")), Ctr(sem("cDVE")), Ctr(sem("cPOOL"))
        dPOOL, dSP, dOUT = Ctr(sem("dPOOL")), Ctr(sem("dSP")), Ctr(sem("dOUT"))

        ones_bf = es.enter_context(nc.sbuf_tensor("ones_bf", [128, 1], BF16))
        eps_sb = es.enter_context(nc.sbuf_tensor("eps_sb", [1, 1], F32))
        ones1 = es.enter_context(nc.sbuf_tensor("ones1", [1, 128], F32))

        block = es.enter_context(nc.Block())

        @block.gpsimd
        def _(g):
            g.memset(ones_bf[:, :], 1.0)
            g.memset(eps_sb[:, :], EPS)
            cPOOL.inc(g.memset(ones1[:, :], 1.0))

        kv_es = ExitStack()
        mask_bf = kv_es.enter_context(nc.sbuf_tensor("mask_bf", [128, 2048], BF16))
        K_sb = kv_es.enter_context(nc.sbuf_tensor("K_sb", [128, HKV * 2048], BF16))
        V_sb = kv_es.enter_context(nc.sbuf_tensor("V_sb", [128, 16 * 1024], BF16))
        p1 = ExitStack()
        wkv_bf = p1.enter_context(nc.sbuf_tensor("wkv_bf", [128, DC * 512], BF16))
        wku_bf = p1.enter_context(nc.sbuf_tensor("wku_bf", [128, 4 * 1024], BF16))
        wvu_bf = p1.enter_context(nc.sbuf_tensor("wvu_bf", [128, 4 * 1024], BF16))
        G_perm = p1.enter_context(nc.sbuf_tensor("G_perm", [128, 4 * 2048], BF16))
        xbf = [p1.enter_context(nc.sbuf_tensor(f"xbf{i}", [128, 1024], BF16)) for i in range(2)]

        g_es = ExitStack()
        G_ps = [g_es.enter_context(nc.psum_tensor(f"G_ps{m}", [128, 1024], F32)) for m in range(4)]

        pe0 = cPE.v
        act0 = cACT.v

        @block.gpsimd
        def _(g):
            g.dma_start(out=mask_bf[:, :], in_=maskd[:, :]).then_inc(dPOOL.sem, 16)
            g.dma_start(out=wkv_bf[:, :], in_=wkv.rearrange("p c m -> p (c m)")).then_inc(dPOOL.sem, 16)
            g.dma_start(out=wku_bf[:, :], in_=wku.rearrange("p j k -> p (j k)")).then_inc(dPOOL.sem, 16)
            g.dma_start(out=wvu_bf[:, :], in_=wvu.rearrange("p j k -> p (j k)")).then_inc(dPOOL.sem, 16)
            for sh in range(2):
                for dc in range(DC):
                    i = sh * DC + dc
                    if i >= 2:
                        g.wait_ge(cPE.sem, pe0 + i - 1)
                    g.dma_start(out=xbf[i % 2][:, :],
                                in_=xt[dc * 128:(dc + 1) * 128, sh * 1024:(sh + 1) * 1024]
                                ).then_inc(dPOOL.sem, 16)
        dp_wkv = dPOOL.v + 16 * 4
        dPOOL.v += 16 * (4 + 64)

        @block.tensor
        def _(t):
            for sh in range(2):
                if sh == 1:
                    t.wait_ge(cACT.sem, act0 + 1)
                for dc in range(DC):
                    i = sh * DC + dc
                    t.wait_ge(dPOOL.sem, dp_wkv + 16 * (i + 1))
                    for mc in range(4):
                        for n in range(2):
                            ins = t.matmul(G_ps[mc][:, n * 512:(n + 1) * 512],
                                           wkv_bf[:, dc * 512 + mc * 128: dc * 512 + (mc + 1) * 128],
                                           xbf[i % 2][:, n * 512:(n + 1) * 512],
                                           start=(dc == 0), stop=(dc == DC - 1))
                    cPE.inc(ins)
        assert cPE.v == pe0 + 64

        @block.scalar
        def _(sc):
            for sh in range(2):
                sc.wait_ge(cPE.sem, pe0 + 32 * (sh + 1))
                for mc in range(4):
                    src = G_ps[mc].rearrange("p (m j) -> p j m", j=8)
                    dst = G_perm[:, mc * 2048:(mc + 1) * 2048].rearrange(
                        "p (j m) -> p j m", m=256)[:, :, sh * 128:(sh + 1) * 128]
                    ins = sc.copy(out=dst, in_=src)
                cACT.inc(ins)
        assert cACT.v == act0 + 2

        g_es.close()
        ku_es = ExitStack()
        kup_ps = [ku_es.enter_context(nc.psum_tensor(f"kup{i}", [128, 512], F32)) for i in range(2)]
        vup_ps = [ku_es.enter_context(nc.psum_tensor(f"vup{i}", [128, 1024], F32)) for i in range(2)]

        pe1 = cPE.v
        act1 = cACT.v

        @block.tensor
        def _(t):
            t.wait_ge(cACT.sem, act0 + 2)
            for h in range(8):
                base = (h % 2) * 64
                mc = h // 2
                for hk in range(HKV):
                    i = h * 8 + hk
                    if i >= 2:
                        t.wait_ge(cACT.sem, act1 + i - 1)
                    for j in range(4):
                        ins = t.matmul(kup_ps[i % 2][:, 0:256],
                                       wku_bf[base:base + 64, j * 1024 + hk * 128: j * 1024 + (hk + 1) * 128],
                                       G_perm[base:base + 64, mc * 2048 + j * 256: mc * 2048 + (j + 1) * 256],
                                       start=(j == 0), stop=(j == 3))
                    cPE.inc(ins)
            for h in range(8):
                base = (h % 2) * 64
                mc = h // 2
                for mb in range(2):
                    i = h * 2 + mb
                    if i >= 2:
                        t.wait_ge(cACT.sem, act1 + 64 + i - 1)
                    for j in range(4):
                        for n in range(2):
                            ins = t.matmul(vup_ps[i % 2][:, n * 512:(n + 1) * 512],
                                           G_perm[base:base + 64,
                                                  mc * 2048 + (4 + j) * 256 + mb * 128:
                                                  mc * 2048 + (4 + j) * 256 + (mb + 1) * 128],
                                           wvu_bf[base:base + 64, j * 1024 + n * 512: j * 1024 + (n + 1) * 512],
                                           start=(j == 0), stop=(j == 3))
                    cPE.inc(ins)
        assert cPE.v == pe1 + 64 + 16

        @block.scalar
        def _(sc):
            for h in range(8):
                for hk in range(HKV):
                    i = h * 8 + hk
                    sc.wait_ge(cPE.sem, pe1 + i + 1)
                    cACT.inc(sc.copy(out=K_sb[:, hk * 2048 + h * 256: hk * 2048 + (h + 1) * 256],
                                     in_=kup_ps[i % 2][:, 0:256]))
            for h in range(8):
                for mb in range(2):
                    i = h * 2 + mb
                    sc.wait_ge(cPE.sem, pe1 + 64 + i + 1)
                    cACT.inc(sc.copy(out=V_sb[:, (h * 2 + mb) * 1024:(h * 2 + mb + 1) * 1024],
                                     in_=vup_ps[i % 2][:, :]))
        assert cACT.v == act1 + 64 + 16
        act_kv_done = cACT.v

        @block.gpsimd
        def _(sy):
            if "G" in taps:
                sy.wait_ge(cACT.sem, act0 + 2)
                sy.dma_start(out=tap_t["G"][:, :], in_=G_perm[:, :]).then_inc(dOUT.sem, 16)
                dOUT.v += 16
            if "K" in taps:
                sy.wait_ge(cACT.sem, act_kv_done)
                sy.dma_start(out=tap_t["K"][:, :], in_=K_sb[:, :]).then_inc(dOUT.sem, 16)
                dOUT.v += 16
            if "V" in taps:
                sy.wait_ge(cACT.sem, act_kv_done)
                sy.dma_start(out=tap_t["V"][:, :], in_=V_sb[:, :]).then_inc(dOUT.sem, 16)
                dOUT.v += 16

        ku_es.close()
        p1.close()

        if upto == "kv":
            @block.sync
            def _(sy):
                sy.wait_ge(dOUT.sem, dOUT.v)
            kv_es.close()
            return nc

        spill_mark = {}
        for HF in range(2):
            a_es = ExitStack()
            ao_sb = a_es.enter_context(nc.sbuf_tensor(f"ao{HF}", [128, HQ * 512], BF16))
            at_es = ExitStack()
            xbf_h = at_es.enter_context(nc.sbuf_tensor(f"xbfh{HF}", [128, DC * 512], BF16))
            wq_bf = [at_es.enter_context(nc.sbuf_tensor(f"wqb{HF}{i}", [128, 2048], BF16)) for i in range(2)]
            q_bf = [at_es.enter_context(nc.sbuf_tensor(f"qb{HF}{i}", [128, 512], BF16)) for i in range(2)]
            P_sb = [at_es.enter_context(nc.sbuf_tensor(f"P{HF}{i}", [128, 512], BF16)) for i in range(2)]
            r_sb = at_es.enter_context(nc.sbuf_tensor(f"r{HF}", [1, 512], F32))
            rb_sb = at_es.enter_context(nc.sbuf_tensor(f"rb{HF}", [128, 512], F32))
            ps_es = ExitStack()
            q_ps = ps_es.enter_context(nc.psum_tensor(f"qps{HF}", [128, 512], F32))
            sc_ps = [ps_es.enter_context(nc.psum_tensor(f"scps{HF}{i}", [128, 512], F32)) for i in range(2)]
            sum_ps = ps_es.enter_context(nc.psum_tensor(f"sps{HF}", [128, 512], F32))
            ao_ps = ps_es.enter_context(nc.psum_tensor(f"aops{HF}", [128, 512], F32))
            rb_ps = ps_es.enter_context(nc.psum_tensor(f"rbps{HF}", [128, 512], F32))

            NSIG = 8 + 8 * HF
            PPE = 4 + 2 * NSIG
            PACT = 2 + NSIG
            PDVE = 10
            pe_a, act_a, dve_a = cPE.v, cACT.v, cDVE.v

            def pe_wq_done(h, half):
                return pe_a + h * PPE + half + 1

            def pe_q_done(h):
                return pe_a + h * PPE + 3

            def pe_qk(h, si):
                return pe_a + h * PPE + 3 + 2 * si + 1

            def pe_pv(h, si):
                return pe_a + h * PPE + 3 + 2 * si + 2

            def pe_head_done(h):
                return pe_a + (h + 1) * PPE

            def act_q_evac(h):
                return act_a + h * PACT + 1

            def act_exp(h, si):
                return act_a + h * PACT + 1 + si + 1

            def act_rb(h):
                return act_a + (h + 1) * PACT

            def dve_mask(h, mi):
                return dve_a + h * PDVE + mi + 1

            def dve_recip(h):
                return dve_a + h * PDVE + 9

            def dve_norm(h):
                return dve_a + (h + 1) * PDVE

            def twof(si):
                tw = 512 - max(0, si // 2 - 4 * HF) * 128
                return tw, 512 - tw

            @block.gpsimd
            def _(g):
                if HF == 0:
                    g.wait_ge(cACT.sem, act_kv_done)
                else:
                    g.wait_ge(dSP.sem, spill_mark[0])
                src = xto.rearrange("(dc p) t -> p dc t", p=128)[:, :, HF * 512:(HF + 1) * 512]
                g.dma_start(out=xbf_h.rearrange("p (dc t) -> p dc t", t=512), in_=src).then_inc(dPOOL.sem, 16)
                for h in range(HQ):
                    for half in range(2):
                        if h >= 1:
                            g.wait_ge(cPE.sem, pe_wq_done(h - 1, half))
                        g.dma_start(out=wq_bf[half][:, :],
                                    in_=wq[h, :, half * 2048:(half + 1) * 2048]).then_inc(dPOOL.sem, 16)
            dp_a = dPOOL.v
            dPOOL.v += 16 * (1 + 2 * HQ)

            def dp_wq(h, half):
                return dp_a + 16 * (1 + h * 2 + half + 1)

            @block.tensor
            def _(t):
                t.wait_ge(cPOOL.sem, 1)
                for h in range(HQ):
                    hk = h // 4
                    for half in range(2):
                        t.wait_ge(dPOOL.sem, dp_wq(h, half))
                        if half == 0 and h >= 1:
                            t.wait_ge(cACT.sem, act_q_evac(h - 1))
                        for d16 in range(16):
                            gdc = half * 16 + d16
                            ins = t.matmul(q_ps[:, :], wq_bf[half][:, d16 * 128:(d16 + 1) * 128],
                                           xbf_h[:, gdc * 512:(gdc + 1) * 512],
                                           start=(gdc == 0), stop=(gdc == DC - 1))
                        cPE.inc(ins)
                    cPE.inc(t.nop())
                    for si in range(NSIG):
                        tw, toff = twof(si)
                        if si >= 2:
                            t.wait_ge(cACT.sem, act_exp(h, si - 2))
                        elif h >= 1:
                            t.wait_ge(cACT.sem, act_exp(h - 1, NSIG - 2 + si))
                        if si == 0:
                            t.wait_ge(cACT.sem, act_q_evac(h))
                        ins = t.matmul(sc_ps[si % 2][:, toff:512],
                                       K_sb[:, hk * 2048 + si * 128: hk * 2048 + (si + 1) * 128],
                                       q_bf[h % 2][:, toff:512], start=True, stop=True)
                        cPE.inc(ins)
                        masked = (8 * HF <= si < 8 * HF + 8)
                        if masked:
                            t.wait_ge(cDVE.sem, dve_mask(h, si - 8 * HF))
                        else:
                            t.wait_ge(cACT.sem, act_exp(h, si))
                        if si == 0 and h >= 1:
                            t.wait_ge(cDVE.sem, dve_norm(h - 1))
                        t.matmul(sum_ps[0:1, toff:512], ones_bf[:, :], P_sb[si % 2][:, toff:512],
                                 start=(si == 0), stop=(si == NSIG - 1))
                        ins = t.matmul(ao_ps[:, toff:512],
                                       V_sb[:, si * 1024 + hk * 128: si * 1024 + (hk + 1) * 128],
                                       P_sb[si % 2][:, toff:512], start=(si == 0), stop=(si == NSIG - 1))
                        cPE.inc(ins)
                    t.wait_ge(cDVE.sem, dve_recip(h))
                    if h >= 1:
                        t.wait_ge(cACT.sem, act_rb(h - 1))
                    ins = t.matmul(rb_ps[:, :], ones1[:, :], r_sb[0:1, :], start=True, stop=True)
                    cPE.inc(ins)
            assert cPE.v == pe_a + HQ * PPE

            @block.scalar
            def _(sc):
                for h in range(HQ):
                    sc.wait_ge(cPE.sem, pe_q_done(h))
                    cACT.inc(sc.copy(out=q_bf[h % 2][:, :], in_=q_ps[:, :]))
                    for si in range(NSIG):
                        tw, toff = twof(si)
                        sc.wait_ge(cPE.sem, pe_qk(h, si))
                        cACT.inc(sc.activation(P_sb[si % 2][:, toff:512], sc_ps[si % 2][:, toff:512],
                                               AF.Exp, scale=SCALE))
                    sc.wait_ge(cPE.sem, pe_head_done(h))
                    cACT.inc(sc.copy(out=rb_sb[:, :], in_=rb_ps[:, :]))
            assert cACT.v == act_a + HQ * PACT

            @block.vector
            def _(v):
                if HF == 0:
                    v.wait_ge(dPOOL.sem, 16)
                for h in range(HQ):
                    for mi in range(8):
                        si = 8 * HF + mi
                        tw, toff = twof(si)
                        v.wait_ge(cACT.sem, act_exp(h, si))
                        cDVE.inc(v.tensor_tensor(P_sb[si % 2][:, toff:toff + 128],
                                                 P_sb[si % 2][:, toff:toff + 128],
                                                 mask_bf[:, si * 128:(si + 1) * 128], OP.mult))
                    v.wait_ge(cPE.sem, pe_pv(h, NSIG - 1))
                    cDVE.inc(v.reciprocal(r_sb[0:1, :], sum_ps[0:1, :]))
                    v.wait_ge(cACT.sem, act_rb(h))
                    cDVE.inc(v.tensor_tensor(ao_sb[:, h * 512:(h + 1) * 512], ao_ps[:, :],
                                             rb_sb[:, :], OP.mult))
            assert cDVE.v == dve_a + HQ * PDVE

            @block.gpsimd
            def _(sy):
                if ("ao%d" % HF) in taps:
                    sy.wait_ge(cDVE.sem, dve_norm(HQ - 1))
                    sy.dma_start(out=tap_t["ao%d" % HF][:, :], in_=ao_sb[:, :]).then_inc(dOUT.sem, 16)
                    dOUT.v += 16

            if upto == ("attn%d" % HF):
                @block.sync
                def _(sy):
                    sy.wait_ge(dOUT.sem, dOUT.v)
                ps_es.close()
                at_es.close()
                a_es.close()
                kv_es.close()
                return nc

            at_es.close()
            ps_es.close()
            wo_es = ExitStack()
            wo_bf = [wo_es.enter_context(nc.sbuf_tensor(f"wob{HF}{i}", [128, 2048], BF16)) for i in range(2)]
            xf32 = [wo_es.enter_context(nc.sbuf_tensor(f"xf{HF}{i}", [128, 512], F32)) for i in range(2)]
            u1_sb = wo_es.enter_context(nc.sbuf_tensor(f"u1_{HF}", [128, DC * 512], F32))
            ubf = [wo_es.enter_context(nc.sbuf_tensor(f"ubf{HF}{i}", [128, 512], BF16)) for i in range(2)]
            u2bf = [wo_es.enter_context(nc.sbuf_tensor(f"u2bf{HF}{i}", [128, 512], BF16)) for i in range(2)]
            stat_sb = wo_es.enter_context(nc.sbuf_tensor(f"stat{HF}", [1, 512 * 5], F32))
            mrb_sb = wo_es.enter_context(nc.sbuf_tensor(f"mrb{HF}", [128, 512], F32))
            rb2_sb = wo_es.enter_context(nc.sbuf_tensor(f"rb2{HF}", [128, 512], F32))
            wops = ExitStack()
            u_ps = [wops.enter_context(nc.psum_tensor(f"ups{HF}{i}", [128, 512], F32)) for i in range(2)]
            stU_ps = wops.enter_context(nc.psum_tensor(f"stU{HF}", [128, 512], F32))
            stU2_ps = wops.enter_context(nc.psum_tensor(f"stU2{HF}", [128, 512], F32))
            br_ps = [wops.enter_context(nc.psum_tensor(f"brps{HF}{i}", [128, 512], F32)) for i in range(2)]

            spill_mark[HF] = _emit_matul_ln(
                nc, block, stage="wo", HF=HF, wsrc=wo, nK=DC, ksz=2048,
                rhs_sb=ao_sb, w_bufs=wo_bf, res_src=("dram", xto), res_bufs=xf32,
                u_sb=u1_sb, ubf=ubf, u2bf=u2bf, stat_sb=stat_sb, mrb_sb=mrb_sb, rbx_sb=rb2_sb,
                y_ps=u_ps, stU_ps=stU_ps, stU2_ps=stU2_ps, br_ps=br_ps,
                ones_bf=ones_bf, ones1=ones1, eps_sb=eps_sb, x1sp=x1sp, yt=None,
                cPE=cPE, cACT=cACT, cDVE=cDVE, cPOOL=cPOOL, dPOOL=dPOOL, dSP=dSP, dOUT=dOUT,
                pre_wait=("dve", dve_norm(HQ - 1)), h_sb=None, taps=taps, tap_t=tap_t)

            @block.sync
            def _(sy):
                if ("x1_%d" % HF) in taps:
                    sy.wait_ge(dSP.sem, spill_mark[HF])
                    sy.dma_start(out=tap_t["x1_%d" % HF][:, :], in_=u1_sb[:, :]).then_inc(dOUT.sem, 16)
                    dOUT.v += 16

            if upto == ("wo%d" % HF):
                @block.sync
                def _(sy):
                    sy.wait_ge(dOUT.sem, dOUT.v)
                wops.close()
                wo_es.close()
                a_es.close()
                kv_es.close()
                return nc

            wops.close()
            wo_es.close()
            a_es.close()

        kv_es.close()

        for HF in range(2):
            f_es = ExitStack()
            h_sb = f_es.enter_context(nc.sbuf_tensor(f"h{HF}", [128, FC * 512], BF16))
            gu_es = ExitStack()
            x1bf = gu_es.enter_context(nc.sbuf_tensor(f"x1bf{HF}", [128, DC * 512], BF16))
            wg_bf = [gu_es.enter_context(nc.sbuf_tensor(f"wgb{HF}{i}", [128, 4096], BF16)) for i in range(2)]
            wu_bf = [gu_es.enter_context(nc.sbuf_tensor(f"wub{HF}{i}", [128, 4096], BF16)) for i in range(2)]
            gs_sb = [gu_es.enter_context(nc.sbuf_tensor(f"gs{HF}{i}", [128, 512], F32)) for i in range(2)]
            gups = ExitStack()
            g_ps = [gups.enter_context(nc.psum_tensor(f"gps{HF}{i}", [128, 512], F32)) for i in range(2)]
            up_ps = [gups.enter_context(nc.psum_tensor(f"upps{HF}{i}", [128, 512], F32)) for i in range(2)]

            pe_f, act_f, dve_f = cPE.v, cACT.v, cDVE.v

            @block.gpsimd
            def _(g):
                g.wait_ge(dSP.sem, spill_mark[1])
                if HF == 1:
                    g.wait_ge(dOUT.sem, dOUT.v)
                g.dma_start(out=x1bf.rearrange("p (dc t) -> p dc t", t=512),
                            in_=x1sp.rearrange("p (dc t) -> p dc t", t=1024)[:, :, HF * 512:(HF + 1) * 512]
                            ).then_inc(dPOOL.sem, 16)
                for fc in range(FC):
                    if fc >= 2:
                        g.wait_ge(cDVE.sem, dve_f + fc - 1)
                    g.dma_start(out=wg_bf[fc % 2][:, :], in_=wg[fc, :, :]).then_inc(dPOOL.sem, 16)
                    g.dma_start(out=wu_bf[fc % 2][:, :], in_=wu[fc, :, :]).then_inc(dPOOL.sem, 16)
            dp_f = dPOOL.v
            dPOOL.v += 16 * (1 + 2 * FC)

            @block.tensor
            def _(t):
                for fc in range(FC):
                    t.wait_ge(dPOOL.sem, dp_f + 16 * (1 + 2 * fc + 2))
                    if fc >= 2:
                        t.wait_ge(cDVE.sem, dve_f + fc - 1)
                    for dc in range(DC):
                        ins = t.matmul(g_ps[fc % 2][:, :], wg_bf[fc % 2][:, dc * 128:(dc + 1) * 128],
                                       x1bf[:, dc * 512:(dc + 1) * 512],
                                       start=(dc == 0), stop=(dc == DC - 1))
                    cPE.inc(ins)
                    for dc in range(DC):
                        ins = t.matmul(up_ps[fc % 2][:, :], wu_bf[fc % 2][:, dc * 128:(dc + 1) * 128],
                                       x1bf[:, dc * 512:(dc + 1) * 512],
                                       start=(dc == 0), stop=(dc == DC - 1))
                    cPE.inc(ins)
            assert cPE.v == pe_f + 2 * FC

            @block.scalar
            def _(sc):
                for fc in range(FC):
                    sc.wait_ge(cPE.sem, pe_f + 2 * fc + 1)
                    if fc >= 2:
                        sc.wait_ge(cDVE.sem, dve_f + fc - 1)
                    cACT.inc(sc.activation(gs_sb[fc % 2][:, :], g_ps[fc % 2][:, :], AF.Silu))
            assert cACT.v == act_f + FC

            @block.vector
            def _(v):
                for fc in range(FC):
                    v.wait_ge(cPE.sem, pe_f + 2 * fc + 2)
                    v.wait_ge(cACT.sem, act_f + fc + 1)
                    cDVE.inc(v.tensor_tensor(h_sb[:, fc * 512:(fc + 1) * 512], up_ps[fc % 2][:, :],
                                             gs_sb[fc % 2][:, :], OP.mult))
            assert cDVE.v == dve_f + FC

            gups.close()
            gu_es.close()

            dn_es = ExitStack()
            wd_bf = [dn_es.enter_context(nc.sbuf_tensor(f"wdb{HF}{i}", [128, 43 * 128], BF16)) for i in range(3)]
            u2_sb = dn_es.enter_context(nc.sbuf_tensor(f"u2_{HF}", [128, DC * 512], F32))
            x1r = [dn_es.enter_context(nc.sbuf_tensor(f"x1r{HF}{i}", [128, 512], F32)) for i in range(2)]
            ubf2 = [dn_es.enter_context(nc.sbuf_tensor(f"ub2{HF}{i}", [128, 512], BF16)) for i in range(2)]
            u2bf2 = [dn_es.enter_context(nc.sbuf_tensor(f"u2b2{HF}{i}", [128, 512], BF16)) for i in range(2)]
            stat2_sb = dn_es.enter_context(nc.sbuf_tensor(f"st2{HF}", [1, 512 * 5], F32))
            mrb2_sb = dn_es.enter_context(nc.sbuf_tensor(f"mrb2{HF}", [128, 512], F32))
            rb3_sb = dn_es.enter_context(nc.sbuf_tensor(f"rb3{HF}", [128, 512], F32))
            dnps = ExitStack()
            y_ps = [dnps.enter_context(nc.psum_tensor(f"yps{HF}{i}", [128, 512], F32)) for i in range(2)]
            stU_ps2 = dnps.enter_context(nc.psum_tensor(f"stUd{HF}", [128, 512], F32))
            stU2_ps2 = dnps.enter_context(nc.psum_tensor(f"stU2d{HF}", [128, 512], F32))
            br_ps2 = [dnps.enter_context(nc.psum_tensor(f"brd{HF}{i}", [128, 512], F32)) for i in range(2)]

            _emit_matul_ln(
                nc, block, stage="down", HF=HF, wsrc=wd, nK=FC, ksz=5504,
                rhs_sb=h_sb, w_bufs=wd_bf, res_src=("x1sp", x1sp), res_bufs=x1r,
                u_sb=u2_sb, ubf=ubf2, u2bf=u2bf2, stat_sb=stat2_sb, mrb_sb=mrb2_sb, rbx_sb=rb3_sb,
                y_ps=y_ps, stU_ps=stU_ps2, stU2_ps=stU2_ps2, br_ps=br_ps2,
                ones_bf=ones_bf, ones1=ones1, eps_sb=eps_sb, x1sp=x1sp, yt=yt,
                cPE=cPE, cACT=cACT, cDVE=cDVE, cPOOL=cPOOL, dPOOL=dPOOL, dSP=dSP, dOUT=dOUT,
                pre_wait=("dve", dve_f + FC), h_sb=h_sb, taps=taps, tap_t=tap_t)
            dOUT.v += 16 * DC

            dnps.close()
            dn_es.close()
            f_es.close()

        @block.sync
        def _(sy):
            sy.wait_ge(dOUT.sem, dOUT.v)

    return nc


def _emit_matul_ln(nc, block, stage, HF, wsrc, nK, ksz, rhs_sb, w_bufs, res_src, res_bufs,
                   u_sb, ubf, u2bf, stat_sb, mrb_sb, rbx_sb, y_ps, stU_ps, stU2_ps, br_ps,
                   ones_bf, ones1, eps_sb, x1sp, yt,
                   cPE, cACT, cDVE, cPOOL, dPOOL, dSP, dOUT, pre_wait, h_sb,
                   taps=(), tap_t=None):
    OPm, OPa, OPs = OP.mult, OP.add, OP.subtract
    pe_b, act_b, dve_b, pool_b = cPE.v, cACT.v, cDVE.v, cPOOL.v
    nhalf = len(w_bufs)
    HB = 2

    @block.gpsimd
    def _(g):
        if stage == "wo":
            g.wait_ge(cDVE.sem, pre_wait[1])
        else:
            g.wait_ge(cDVE.sem, pre_wait[1])
        for c in range(DC):
            for hb in range(HB):
                i = c * HB + hb
                if i >= nhalf:
                    j = i - nhalf
                    g.wait_ge(cPE.sem, pe_b + j + 1)
                if stage == "wo":
                    g.dma_start(out=w_bufs[i % nhalf][:, :],
                                in_=wsrc[c, :, hb * ksz:(hb + 1) * ksz]).then_inc(dPOOL.sem, 16)
                else:
                    g.dma_start(out=w_bufs[i % nhalf][:, :],
                                in_=wsrc[c, :, hb * ksz:(hb + 1) * ksz]).then_inc(dPOOL.sem, 16)
            if c >= 2:
                g.wait_ge(cACT.sem, act_b + (c - 2) + 1)
                cPOOL.inc(g.tensor_tensor(u2bf[(c - 2) % 2][:, :], ubf[(c - 2) % 2][:, :],
                                          ubf[(c - 2) % 2][:, :], OPm))
        for k in (DC - 2, DC - 1):
            g.wait_ge(cACT.sem, act_b + k + 1)
            cPOOL.inc(g.tensor_tensor(u2bf[k % 2][:, :], ubf[k % 2][:, :], ubf[k % 2][:, :], OPm))
    dp_b = dPOOL.v
    dPOOL.v += 16 * HB * DC

    @block.sync
    def _(sy):
        sy.wait_ge(cDVE.sem, pre_wait[1])
        for c in range(DC):
            if c >= 2:
                sy.wait_ge(cDVE.sem, dve_b + (c - 2) + 1)
            if res_src[0] == "dram":
                sy.dma_start(out=res_bufs[c % 2][:, :],
                             in_=res_src[1][c * 128:(c + 1) * 128, HF * 512:(HF + 1) * 512]
                             ).then_inc(dSP.sem, 16)
            else:
                sy.dma_start(out=res_bufs[c % 2][:, :],
                             in_=res_src[1][:, c * 1024 + HF * 512: c * 1024 + HF * 512 + 512]
                             ).then_inc(dSP.sem, 16)
    dsp_b = dSP.v
    dSP.v += 16 * DC

    nk_half = nK // HB if stage == "wo" else 43

    @block.tensor
    def _(t):
        if h_sb is not None:
            t.wait_ge(cDVE.sem, pre_wait[1])
        for c in range(DC):
            if c >= 2:
                t.wait_ge(cDVE.sem, dve_b + (c - 2) + 1)
            for hb in range(HB):
                t.wait_ge(dPOOL.sem, dp_b + 16 * (c * HB + hb + 1))
                for kk in range(nk_half):
                    fk = hb * nk_half + kk
                    ins = t.matmul(y_ps[c % 2][:, :], w_bufs[(c * HB + hb) % nhalf][:, kk * 128:(kk + 1) * 128],
                                   rhs_sb[:, fk * 512:(fk + 1) * 512],
                                   start=(fk == 0), stop=(fk == nK - 1))
                if hb == 0:
                    cPE.inc(ins)
            if c >= 2:
                k = c - 2
                t.wait_ge(cPOOL.sem, pool_b + k + 1)
                t.matmul(stU_ps[0:1, :], ones_bf[:, :], ubf[k % 2][:, :],
                         start=(k == 0), stop=(k == DC - 1))
                ins = t.matmul(stU2_ps[0:1, :], ones_bf[:, :], u2bf[k % 2][:, :],
                               start=(k == 0), stop=(k == DC - 1))
            cPE.inc(ins)
        for k in (DC - 2, DC - 1):
            t.wait_ge(cPOOL.sem, pool_b + k + 1)
            t.matmul(stU_ps[0:1, :], ones_bf[:, :], ubf[k % 2][:, :],
                     start=(k == 0), stop=(k == DC - 1))
            ins = t.matmul(stU2_ps[0:1, :], ones_bf[:, :], u2bf[k % 2][:, :],
                           start=(k == 0), stop=(k == DC - 1))
            if k == DC - 1:
                cPE.inc(ins)
        t.wait_ge(cACT.sem, act_b + DC + 4)
        ins = t.matmul(br_ps[0][:, :], ones1[:, :], stat_sb[0:1, 2048:2560], start=True, stop=True)
        cPE.inc(ins)
        ins = t.matmul(br_ps[1][:, :], ones1[:, :], stat_sb[0:1, 1024:1536], start=True, stop=True)
        cPE.inc(ins)
    assert cPE.v == pe_b + 2 * DC + 1 + 2, (cPE.v, pe_b)
    pe_stats_done = pe_b + 2 * DC + 1
    pe_br = (pe_b + 2 * DC + 2, pe_b + 2 * DC + 3)

    @block.vector
    def _(v):
        for c in range(DC):
            v.wait_ge(cPE.sem, pe_b + 2 * c + 2)
            v.wait_ge(dSP.sem, dsp_b + 16 * (c + 1))
            cDVE.inc(v.tensor_tensor(u_sb[:, c * 512:(c + 1) * 512], y_ps[c % 2][:, :],
                                     res_bufs[c % 2][:, :], OPa))
    assert cDVE.v == dve_b + DC

    @block.scalar
    def _(sc):
        for c in range(DC):
            sc.wait_ge(cDVE.sem, dve_b + c + 1)
            if c >= 2:
                sc.wait_ge(cPOOL.sem, pool_b + (c - 2) + 1)
            cACT.inc(sc.copy(out=ubf[c % 2][:, :], in_=u_sb[:, c * 512:(c + 1) * 512]))
        sc.wait_ge(cPE.sem, pe_stats_done)
        sc.mul(stat_sb[0:1, 0:512], stU_ps[0:1, :], 1.0 / DM)
        cACT.inc(sc.mul(stat_sb[0:1, 1024:1536], stU2_ps[0:1, :], 1.0 / DM))
    assert cACT.v == act_b + DC + 1

    @block.vector
    def _(v):
        v.wait_ge(cACT.sem, act_b + DC + 1)
        cDVE.inc(v.tensor_tensor(stat_sb[0:1, 512:1024], stat_sb[0:1, 0:512],
                                 stat_sb[0:1, 0:512], OPm))
    assert cDVE.v == dve_b + DC + 1

    @block.gpsimd
    def _(g):
        g.wait_ge(cDVE.sem, dve_b + DC + 1)
        cPOOL.inc(g.tensor_tensor(stat_sb[0:1, 1536:2048], stat_sb[0:1, 1024:1536],
                                  stat_sb[0:1, 512:1024], OPs))
    assert cPOOL.v == pool_b + DC + 1

    @block.scalar
    def _(sc):
        sc.wait_ge(cPOOL.sem, pool_b + DC + 1)
        cACT.inc(sc.activation(stat_sb[0:1, 512:1024], stat_sb[0:1, 1536:2048],
                               AF.Ln, bias=eps_sb[0:1, 0:1], scale=1.0))
    assert cACT.v == act_b + DC + 2

    @block.vector
    def _(v):
        v.wait_ge(cACT.sem, act_b + DC + 2)
        cDVE.inc(v.tensor_copy(stat_sb[0:1, 1536:2048], stat_sb[0:1, 512:1024]))
    assert cDVE.v == dve_b + DC + 2

    @block.scalar
    def _(sc):
        sc.wait_ge(cDVE.sem, dve_b + DC + 2)
        cACT.inc(sc.activation(stat_sb[0:1, 1024:1536], stat_sb[0:1, 1536:2048],
                               AF.Exp, scale=-0.5))
    assert cACT.v == act_b + DC + 3

    @block.vector
    def _(v):
        v.wait_ge(cACT.sem, act_b + DC + 3)
        cDVE.inc(v.tensor_tensor(stat_sb[0:1, 512:1024], stat_sb[0:1, 0:512],
                                 stat_sb[0:1, 1024:1536], OPm))
    assert cDVE.v == dve_b + DC + 3

    @block.scalar
    def _(sc):
        sc.wait_ge(cDVE.sem, dve_b + DC + 3)
        cACT.inc(sc.mul(stat_sb[0:1, 2048:2560], stat_sb[0:1, 512:1024], -1.0))
        sc.wait_ge(cPE.sem, pe_br[0])
        cACT.inc(sc.copy(out=mrb_sb[:, :], in_=br_ps[0][:, :]))
        sc.wait_ge(cPE.sem, pe_br[1])
        cACT.inc(sc.copy(out=rbx_sb[:, :], in_=br_ps[1][:, :]))
    assert cACT.v == act_b + DC + 6

    dve_app = cDVE.v
    u1tap = "u1pre_%d" % HF
    stattap = "stat_%d" % HF
    tapmark = [None]

    @block.sync
    def _(sy):
        if u1tap in taps and yt is None:
            sy.wait_ge(cDVE.sem, dve_b + DC)
            sy.dma_start(out=tap_t[u1tap][:, :], in_=u_sb[:, :]).then_inc(dOUT.sem, 16)
            dOUT.v += 16
            tapmark[0] = dOUT.v
        if stattap in taps and yt is None:
            sy.wait_ge(cACT.sem, act_b + DC + 6)
            sy.dma_start(out=tap_t[stattap][:, :], in_=stat_sb[:, :]).then_inc(dOUT.sem, 16)
            dOUT.v += 16

    @block.vector
    def _(v):
        v.wait_ge(cACT.sem, act_b + DC + 6)
        if tapmark[0] is not None:
            v.wait_ge(dOUT.sem, tapmark[0])
        for c in range(DC):
            cDVE.inc(v.tensor_tensor(u_sb[:, c * 512:(c + 1) * 512], u_sb[:, c * 512:(c + 1) * 512],
                                     rbx_sb[:, :], OPm))
    assert cDVE.v == dve_app + DC

    @block.gpsimd
    def _(g):
        for c in range(DC):
            g.wait_ge(cDVE.sem, dve_app + c + 1)
            cPOOL.inc(g.tensor_tensor(u_sb[:, c * 512:(c + 1) * 512], u_sb[:, c * 512:(c + 1) * 512],
                                      mrb_sb[:, :], OPa))
    assert cPOOL.v == pool_b + 2 * DC + 1

    @block.sync
    def _(sy):
        if yt is None:
            sy.wait_ge(cPOOL.sem, pool_b + 2 * DC + 1)
            sy.dma_start(out=x1sp.rearrange("p (dc t) -> p dc t", t=1024)[:, :, HF * 512:(HF + 1) * 512],
                         in_=u_sb.rearrange("p (dc t) -> p dc t", t=512)).then_inc(dSP.sem, 16)
        else:
            for c in range(DC):
                sy.wait_ge(cPOOL.sem, pool_b + DC + 1 + c + 1)
                sy.dma_start(out=yt[c, :, HF * 512:(HF + 1) * 512],
                             in_=u_sb[:, c * 512:(c + 1) * 512]).then_inc(dOUT.sem, 16)
    if yt is None:
        dSP.v += 16
        return dSP.v
    return None



def _pack_inputs(L0_attn_hidden_states, L0_attn_Wq, L0_attn_Wk_down, L0_attn_Wv_down,
                 L0_attn_Wk_up, L0_attn_Wv_up, L0_attn_Wo,
                 L0_ffn_W_gate, L0_ffn_W_up, L0_ffn_W_down):
    f32 = np.float32
    x = np.asarray(L0_attn_hidden_states, f32)
    Wq = np.asarray(L0_attn_Wq, f32)
    Wkd = np.asarray(L0_attn_Wk_down, f32)
    Wvd = np.asarray(L0_attn_Wv_down, f32)
    Wku = np.asarray(L0_attn_Wk_up, f32)
    Wvu = np.asarray(L0_attn_Wv_up, f32)
    Wo = np.asarray(L0_attn_Wo, f32)
    Wg = np.asarray(L0_ffn_W_gate, f32)
    Wu = np.asarray(L0_ffn_W_up, f32)
    Wd = np.asarray(L0_ffn_W_down, f32)

    xt = [np.ascontiguousarray(x[b].T) for b in range(B)]

    G6 = np.empty((128, DC, 8, 64), f32)
    G6[:, :, :, :32] = Wkd.reshape(8, 32, DC, 128).transpose(3, 2, 0, 1)
    G6[:, :, :, 32:] = Wvd.reshape(8, 32, DC, 128).transpose(3, 2, 0, 1)
    wkv = np.ascontiguousarray(G6.reshape(128, DC, 512))

    def dup_up(W):
        A = np.ascontiguousarray(W.T).reshape(4, 64, 1024)
        return np.ascontiguousarray(np.concatenate([A, A], axis=1).transpose(1, 0, 2))

    wku = dup_up(Wku)
    wvu = dup_up(Wvu)
    wq = np.ascontiguousarray(Wq.reshape(HQ, 128, DC, 128).transpose(0, 3, 2, 1).reshape(HQ, 128, DM))
    wop = np.ascontiguousarray(Wo.reshape(DC, 128, DC, 128).transpose(0, 3, 2, 1).reshape(DC, 128, DM))
    wgp = np.ascontiguousarray(Wg.reshape(FC, 128, DC, 128).transpose(0, 3, 2, 1).reshape(FC, 128, DM))
    wup = np.ascontiguousarray(Wu.reshape(FC, 128, DC, 128).transpose(0, 3, 2, 1).reshape(FC, 128, DM))
    wdp = np.ascontiguousarray(Wd.reshape(DC, 128, FC, 128).transpose(0, 3, 2, 1).reshape(DC, 128, DFF))

    masks = []
    tri = (np.arange(128)[:, None] <= np.arange(128)[None, :]).astype(f32)
    for par in range(2):
        tcmap = PAIR_SLOTS[par]
        M = np.zeros((128, 16, 128), f32)
        for sg in range(16):
            tcs = tcmap[sg // 2]
            if tcs > sg:
                M[:, sg, :] = 1.0
            elif tcs == sg:
                M[:, sg, :] = tri
        masks.append(np.ascontiguousarray(M.reshape(128, 2048)))

    xto = []
    for c in range(8):
        b, par = c // 2, c % 2
        cols = np.concatenate([np.arange(tc * 128, (tc + 1) * 128) for tc in PAIR_SLOTS[par]])
        xto.append(np.ascontiguousarray(xt[b][:, cols]))

    in_maps = []
    for c in range(8):
        b, par = c // 2, c % 2
        in_maps.append({
            "xt": xt[b], "xto": xto[c], "wkv": wkv, "wku": wku, "wvu": wvu,
            "wq": wq, "wo": wop, "wg": wgp, "wu": wup, "wd": wdp, "mask": masks[par],
        })
    return in_maps


def _assemble(results):
    out = np.empty((B, S, DM), np.float32)
    for c in range(8):
        b, par = c // 2, c % 2
        arr = results[c]["yt"].reshape(DM, 1024).T
        for k, tc in enumerate(PAIR_SLOTS[par]):
            out[b, tc * 128:(tc + 1) * 128, :] = arr[k * 128:(k + 1) * 128]
    return out


def kernel(**inputs):
    in_maps = _pack_inputs(**inputs)
    nc = build_program()
    res = run_bass_kernel_spmd(nc, in_maps, core_ids=list(range(8)))
    return _assemble(res.results)
